# version 23
# baseline (speedup 1.0000x reference)
import contextlib
import math
import sys

sys.path.insert(0, '/opt/trn_rl_repo')
import numpy as np
import concourse.bass as bass
import concourse.mybir as mybir
from concourse.bass_utils import run_bass_kernel_spmd

F32 = mybir.dt.float32
BF16 = mybir.dt.bfloat16
I32 = mybir.dt.int32
AX = mybir.AxisListType.X
AF = mybir.ActivationFunctionType
OP = mybir.AluOpType

D = 4096
NCH = 32
N_CORES = 8
Q_SCALE = 128.0 ** -0.5

ENGINES = ("tensor", "vector", "scalar", "gpsimd", "sync")
SELF_GUARDED = ("vector", "scalar", "gpsimd")


def _alibi_slopes(n_heads):
    closest_pow2 = 2 ** math.floor(math.log2(n_heads))
    base = 2.0 ** (-(2.0 ** (-(math.log2(closest_pow2) - 3))))
    slopes = [base ** i for i in range(1, closest_pow2 + 1)]
    if closest_pow2 != n_heads:
        extra_base = 2.0 ** (-(2.0 ** (-(math.log2(2 * closest_pow2) - 3))))
        n_extra = min(closest_pow2, n_heads - closest_pow2)
        slopes += [extra_base ** i for i in range(1, 2 * n_extra + 1, 2)]
    return slopes


class SemCounter:
    __slots__ = ("handle", "cur", "name", "owner")

    def __init__(self, handle, name):
        self.handle = handle
        self.cur = 0
        self.name = name
        self.owner = None


class Prog:
    def __init__(self, nc, ctx):
        self.nc = nc
        self.ctx = ctx
        self.q = {e: [] for e in ENGINES}
        self.esem = {e: self.sem("sem_" + e) for e in ("tensor", "vector", "scalar", "gpsimd")}
        self.all_store_sems = []

    def sem(self, name):
        h = self.ctx.enter_context(self.nc.semaphore(name))
        return SemCounter(h, name)

    def _push(self, eng, waits, fn, inc, inc_n):
        def emit(engine):
            for s, v in waits:
                if v > 0:
                    engine.wait_ge(s.handle, v)
            ins = fn(engine)
            if inc is not None:
                assert ins is not None, "inc step must return an instruction"
                ins.then_inc(inc.handle, inc_n)
        self.q[eng].append(emit)

    def step(self, eng, fn, wait=()):
        es = self.esem[eng]
        waits = [(s, v) for (s, v) in wait]
        if eng in SELF_GUARDED and es.cur > 0:
            waits.append((es, es.cur))
        es.owner = eng
        es.cur += 1
        self._push(eng, waits, fn, es, 1)
        return es.cur

    def dma(self, queue, fn, sem, wait=()):
        if sem.owner is None:
            sem.owner = "dma"
        assert sem.owner == "dma", f"sem {sem.name} is a compute sem"
        sem.cur += 16
        self._push(queue, [(s, v) for (s, v) in wait], fn, sem, 16)
        return sem.cur

    def wait(self, eng, sem, val):
        self._push(eng, [(sem, val)], lambda e: None, None, None)

    def barrier(self, dma_sems=()):
        targets = [(s, s.cur) for s in self.esem.values() if s.cur > 0]
        targets += [(s, s.cur) for s in dma_sems if s.cur > 0]
        for eng in ENGINES:
            for s, v in targets:
                if s is self.esem.get(eng):
                    continue
                self.wait(eng, s, v)



def _phase_ln(p, sb, dram, src_name, dst_name, T, tag):
    nT = T // 128
    SV, SS = p.esem["vector"], p.esem["scalar"]
    X, XN = dram[src_name], dram[dst_name]

    xbuf = [sb(f"{tag}x{i}", [128, D], F32) for i in range(2)]
    xnbuf = [sb(f"{tag}xn{i}", [128, D], BF16) for i in range(2)]
    st = [sb(f"{tag}st{i}", [128, 16], F32) for i in range(2)]
    DXA = [p.sem(f"{tag}DXA{i}") for i in range(2)]
    DST = [p.sem(f"{tag}DST{i}") for i in range(2)]
    aff_ms = [0, 0]
    sq_ms = [0, 0]
    tile_ms = []

    for it in range(nT):
        par = it % 2
        w = []
        if aff_ms[par]:
            w = [(SV, aff_ms[par]), (SS, sq_ms[par])]
        d_x = p.dma("gpsimd", lambda g, it=it, par=par: g.dma_start(
            out=xbuf[par][:], in_=X[it * 128:(it + 1) * 128, :]), DXA[par], wait=w)
        sq_ms[par] = p.step("scalar", lambda sc, par=par: sc.activation(
            xnbuf[par][:], xbuf[par][:], AF.Square, accum_out=st[par][:, 1:2]),
            wait=[(DXA[par], d_x), (DST[par], DST[par].cur)])
        p.step("vector", lambda v, par=par: v.reduce_sum(
            st[par][:, 0:1], xbuf[par][:], axis=AX), wait=[(DXA[par], d_x)])
        p.step("vector", lambda v, par=par: v.tensor_scalar(
            st[par][:, 3:4], st[par][:, 0:1], 1.0 / D, None, OP.mult))
        p.step("vector", lambda v, par=par: v.tensor_scalar(
            st[par][:, 4:5], st[par][:, 1:2], 1.0 / D, None, OP.mult),
            wait=[(SS, sq_ms[par])])
        p.step("vector", lambda v, par=par: v.tensor_mul(
            st[par][:, 5:6], st[par][:, 3:4], st[par][:, 3:4]))
        vv = p.step("vector", lambda v, par=par: v.tensor_sub(
            st[par][:, 6:7], st[par][:, 4:5], st[par][:, 5:6]))
        s_std = p.step("scalar", lambda sc, par=par: sc.activation(
            st[par][:, 7:8], st[par][:, 6:7], AF.Sqrt, bias=np.float32(1e-5)), wait=[(SV, vv)])
        p.step("vector", lambda v, par=par: v.reciprocal(
            st[par][:, 8:9], st[par][:, 7:8]), wait=[(SS, s_std)])
        aff_ms[par] = p.step("vector", lambda v, par=par: v.tensor_scalar(
            xnbuf[par][:], xbuf[par][:], st[par][:, 3:4], st[par][:, 8:9],
            OP.subtract, OP.mult), wait=[(DST[par], DST[par].cur)])
        p.dma("sync", lambda s, it=it, par=par: s.dma_start(
            out=XN[it * 128:(it + 1) * 128, :], in_=xnbuf[par][:]),
            DST[par], wait=[(SV, aff_ms[par])])
        tile_ms.append((DST[par], DST[par].cur))
    return [DXA[0], DXA[1], DST[0], DST[1]], tile_ms


def _phase_kv(p, sb, psa, dram, T, wkvb, kT, vtok, ln_ms):
    SP, SV = p.esem["tensor"], p.esem["vector"]
    XN = dram["XN"]
    NB = (T + 511) // 512
    DTR = p.sem("KV_DTR")
    xnTb = sb("KVxnT", [128, NCH * 512], BF16, is_transpose_dest=True)
    psK = psa("KVpsK", [128, 512])
    psV = [psa(f"KVpsV{i}", [128, 512]) for i in range(2)]

    pe_blk = 0
    ev_ms = 0
    for blk in range(NB):
        bs = min(512, T - blk * 512)
        ntt = bs // 128
        last_tile = min(len(ln_ms) - 1, blk * 4 + ntt - 1)
        w = [ln_ms[last_tile]]
        if last_tile >= 1:
            w.append(ln_ms[last_tile - 1])
        if blk:
            w.append((SP, pe_blk))
        d_tr = DTR.cur
        for c in range(NCH):
            d_tr = p.dma("sync", lambda s, c=c, blk=blk, bs=bs: s.dma_start(
                out=xnTb[:, c * 512:c * 512 + bs],
                in_=XN[blk * 512:blk * 512 + bs, c * 128:(c + 1) * 128],
                transpose=True), DTR, wait=w if c == 0 else [])
        for c in range(NCH):
            p.step("tensor", lambda t, c=c, bs=bs: t.matmul(
                psK[:, 0:bs], wkvb[:, c * 256:c * 256 + 128],
                xnTb[:, c * 512:c * 512 + bs],
                start=(c == 0), stop=(c == NCH - 1)),
                wait=[(DTR, d_tr), (SV, ev_ms)] if c == 0 else [])
        spk = SP.cur
        p.step("vector", lambda v, blk=blk, bs=bs: v.tensor_copy(
            kT[:, blk * 512:blk * 512 + bs], psK[:, 0:bs]), wait=[(SP, spk)])
        for tt in range(ntt):
            vp = tt % 2
            for c in range(NCH):
                p.step("tensor", lambda t, c=c, tt=tt, vp=vp: t.matmul(
                    psV[vp][:, 0:128],
                    xnTb[:, c * 512 + tt * 128: c * 512 + (tt + 1) * 128],
                    wkvb[:, c * 256 + 128:c * 256 + 256],
                    start=(c == 0), stop=(c == NCH - 1)),
                    wait=[(SV, SV.cur)] if c == 0 else [])
            spv = SP.cur
            p.step("vector", lambda v, blk=blk, tt=tt, vp=vp: v.tensor_copy(
                vtok[:, (blk * 4 + tt) * 128:(blk * 4 + tt + 1) * 128],
                psV[vp][:, 0:128]), wait=[(SP, spv)])
        ev_ms = SV.cur
        pe_blk = SP.cur
    return [DTR]


def _phase_proj(p, sb, psa, dram, T, tag, src, wname, out_hbm, epilogue,
                transpose_weight=False, extra_mm=None, evac_bufs=2,
                interleave=None):
    SP = p.esem["tensor"]
    TH = T // 2
    nTT = TH // 128
    SRC = dram[src]
    W = dram[wname]
    DXH = p.sem(f"{tag}DXH")
    DW = [p.sem(f"{tag}DW{i}") for i in range(2)]
    xnT = sb(f"{tag}xnT", [128, NCH * TH], BF16, is_transpose_dest=True)
    wsl = [sb(f"{tag}w{i}", [128, NCH * 512], BF16, is_transpose_dest=True)
           for i in range(2)]
    ps = [psa(f"{tag}ps{i}", [128, 512]) for i in range(4)]

    pe_th = 0
    pe_os = [0, 0]
    bank_free = [(None, 0)] * 4
    for th in range(2):
        d_xh = DXH.cur
        for c in range(NCH):
            d_xh = p.dma("sync", lambda s, c=c, th=th: s.dma_start(
                out=xnT[:, c * TH:(c + 1) * TH],
                in_=SRC[th * TH:(th + 1) * TH, c * 128:(c + 1) * 128],
                transpose=True), DXH,
                wait=[(SP, pe_th)] if c == 0 else [])
        for os in range(8):
            osg = th * 8 + os
            par = os % 2
            w = [(SP, pe_os[par])] if pe_os[par] else []
            if not transpose_weight:
                d_w = p.dma("gpsimd", lambda g, os=os, par=par: g.dma_start(
                    out=wsl[par][:].rearrange("p (k o) -> p k o", o=512),
                    in_=W[:, os * 512:(os + 1) * 512].rearrange(
                        "(k p) o -> p k o", p=128)), DW[par], wait=w)
            else:
                d_w = DW[par].cur
                for c in range(NCH):
                    d_w = p.dma("sync", lambda s, os=os, par=par, c=c: s.dma_start(
                        out=wsl[par][:, c * 512:(c + 1) * 512],
                        in_=W[os * 512:(os + 1) * 512, c * 128:(c + 1) * 128],
                        transpose=True), DW[par], wait=w if c == 0 else [])
            for tt in range(nTT):
                b = (osg * nTT + tt) % 4
                fs, fv = bank_free[b]
                w0 = [(DW[par], d_w), (DXH, d_xh)]
                if fs is not None:
                    w0.append((fs, fv))
                for c in range(NCH):
                    last = (c == NCH - 1) and extra_mm is None
                    p.step("tensor", lambda t, c=c, tt=tt, par=par, b=b, last=last: t.matmul(
                        ps[b][:], xnT[:, c * TH + tt * 128: c * TH + (tt + 1) * 128],
                        wsl[par][:, c * 512:(c + 1) * 512],
                        start=(c == 0), stop=last),
                        wait=w0 if c == 0 else [])
                if extra_mm is not None:
                    p.step("tensor", lambda t, th=th, os=os, tt=tt, b=b: extra_mm(
                        t, th, os, tt, ps[b][:]))
                bank_free[b] = epilogue(p, th, os, tt, ps[b][:], SP.cur)
            pe_os[par] = SP.cur
            if interleave is not None:
                for _ in range(8):
                    interleave()
        pe_th = SP.cur
    return [DXH, DW[0], DW[1]]


def _phase_proj_fm(p, sb, psa, dram, T, tag, src, wname, out_hbm, scale,
                   interleave=None, ln_ms=None):
    SP, SS = p.esem["tensor"], p.esem["scalar"]
    TH = T // 2
    nTB = (TH + 511) // 512
    SRC = dram[src]
    W = dram[wname]
    OUT = dram[out_hbm]
    DXH = p.sem(f"{tag}DXH")
    DW = [p.sem(f"{tag}DW{i}") for i in range(2)]
    DST = [p.sem(f"{tag}DO{i}") for i in range(2)]
    xnT = sb(f"{tag}xnT", [128, NCH * TH], BF16, is_transpose_dest=True)
    wsl = [sb(f"{tag}w{i}", [128, NCH * 128], BF16) for i in range(2)]
    obuf = [sb(f"{tag}ob{i}", [128, 512], BF16) for i in range(2)]
    ps = [psa(f"{tag}ps{i}", [128, 512]) for i in range(4)]

    pe_th = 0
    pe_o = [0, 0]
    bank_free = [0, 0, 0, 0]
    ob_free = [0, 0]
    for th in range(2):
        d_xh = DXH.cur
        for c in range(NCH):
            w0 = [(SP, pe_th)] if pe_th else []
            if c == 0 and ln_ms is not None:
                lt = min(len(ln_ms) - 1, (th + 1) * (TH // 128) - 1)
                w0 = w0 + [ln_ms[lt], ln_ms[lt - 1]]
            d_xh = p.dma("sync", lambda s, c=c, th=th: s.dma_start(
                out=xnT[:, c * TH:(c + 1) * TH],
                in_=SRC[th * TH:(th + 1) * TH, c * 128:(c + 1) * 128],
                transpose=True), DXH,
                wait=w0 if c == 0 else [])
        for o in range(NCH):
            par = o % 2
            w = [(SP, pe_o[par])] if pe_o[par] else []
            d_w = p.dma("gpsimd", lambda g, o=o, par=par: g.dma_start(
                out=wsl[par][:].rearrange("p (c q) -> p c q", q=128),
                in_=W[:, o * 128:(o + 1) * 128].rearrange(
                    "(c p) q -> p c q", p=128)), DW[par], wait=w)
            for k in range(NCH):
                for tb in range(nTB):
                    tw = min(512, TH - tb * 512)
                    p.step("tensor", lambda t, k=k, tb=tb, par=par, o=o, tw=tw: t.matmul(
                        ps[2 * (o % 2) + tb][:, 0:tw],
                        wsl[par][:, k * 128:(k + 1) * 128],
                        xnT[:, k * TH + tb * 512: k * TH + tb * 512 + tw],
                        start=(k == 0), stop=(k == NCH - 1)),
                        wait=([(DW[par], d_w), (DXH, d_xh),
                               (SS, bank_free[2 * (o % 2) + tb])]
                              if k == 0 and tb == 0 else
                              ([(SS, bank_free[2 * (o % 2) + tb])]
                               if k == 0 else [])))
            mm = SP.cur
            for tb in range(nTB):
                tw = min(512, TH - tb * 512)
                bpar = (o * nTB + tb) % 2
                ev = p.step("scalar", lambda sc, tb=tb, o=o, bpar=bpar, tw=tw: sc.activation(
                    obuf[bpar][:, 0:tw], ps[2 * (o % 2) + tb][:, 0:tw],
                    AF.Copy, scale=np.float32(scale)),
                    wait=[(SP, mm), (DST[bpar], ob_free[bpar])])
                bank_free[2 * (o % 2) + tb] = ev
                col = th * TH + tb * 512
                p.dma("scalar", lambda s, o=o, col=col, bpar=bpar, tw=tw: s.dma_start(
                    out=OUT[o * 128:(o + 1) * 128, col:col + tw],
                    in_=obuf[bpar][:, 0:tw]), DST[bpar], wait=[(SS, ev)])
                ob_free[bpar] = DST[bpar].cur
            pe_o[par] = SP.cur
            if interleave is not None:
                for _ in range(2):
                    interleave()
        pe_th = SP.cur
    return [DXH, DW[0], DW[1], DST[0], DST[1]]


def _phase_b0_maker(p, sb, psa, dram, sTt, z8t):
    SV = p.esem["vector"]
    WQQ, WDQ = dram["wqq"], dram["WDQ"]
    NB0 = 4
    DQI = [p.sem(f"B0DQI{i}") for i in range(NB0)]
    DQO = [p.sem(f"B0DQO{i}") for i in range(2)]
    cvt = [sb(f"B0cvt{i}", [128, 1024], I32) for i in range(NB0)]
    cvf = [sb(f"B0cvf{i}", [128, 1024], F32) for i in range(2)]
    wdq = [sb(f"B0wdq{i}", [128, 1024], BF16) for i in range(2)]
    state = {"i": 0}
    dq_ms = [0] * NB0
    cv_ms = [0, 0]

    def emit_group():
        i = state["i"]
        if i >= 128:
            return False
        state["i"] += 1
        c, qt = i // 4, i % 4
        par = i % NB0
        wpar = i % 2
        w = [(SV, dq_ms[par])] if dq_ms[par] else []
        d_i = p.dma("gpsimd", lambda g, c=c, qt=qt, par=par: g.dma_start(
            out=cvt[par][:], in_=WQQ[c * 128:(c + 1) * 128,
                                     qt * 1024:(qt + 1) * 1024]),
            DQI[par], wait=w)
        g0 = c * 32 + qt * 8
        p.step("vector", lambda v, par=par, wpar=wpar, g0=g0: v.tensor_tensor(
            cvf[wpar][:].rearrange("p (g c) -> p g c", c=128),
            cvt[par][:].rearrange("p (g c) -> p g c", c=128),
            sTt[:, g0:g0 + 8].rearrange("p (g o) -> p g o", o=1
                                        ).broadcast_to((128, 8, 128)),
            OP.mult),
            wait=[(DQI[par], d_i), (SV, cv_ms[wpar])])
        p.step("vector", lambda v, wpar=wpar, g0=g0: v.tensor_tensor(
            wdq[wpar][:].rearrange("p (g c) -> p g c", c=128),
            cvf[wpar][:].rearrange("p (g c) -> p g c", c=128),
            z8t[:, g0:g0 + 8].rearrange("p (g o) -> p g o", o=1
                                        ).broadcast_to((128, 8, 128)),
            OP.add),
            wait=[(DQO[wpar], DQO[wpar].cur)])
        dq_ms[par] = SV.cur
        cv_ms[wpar] = SV.cur
        p.dma("sync", lambda s, c=c, qt=qt, wpar=wpar: s.dma_start(
            out=WDQ[c * 128:(c + 1) * 128, qt * 1024:(qt + 1) * 1024],
            in_=wdq[wpar][:]), DQO[wpar], wait=[(SV, dq_ms[par])])
        return True

    NT = 4
    DWT = [p.sem(f"B0DWT{i}") for i in range(NT)]
    DWTS = [p.sem(f"B0DWTS{i}") for i in range(NT)]
    tbuf = [sb(f"B0tb{i}", [128, 512], BF16, is_transpose_dest=True)
            for i in range(NT)]
    tstate = {"j": 0, "gated": False}

    def emit_wdqt():
        j = tstate["j"]
        if j >= 256:
            return False
        tstate["j"] += 1
        osl, c = j // 32, j % 32
        par = j % NT
        w = [(DWTS[par], DWTS[par].cur)]
        if not tstate["gated"]:
            w += [(DQO[0], DQO[0].cur), (DQO[1], DQO[1].cur)]
            tstate["gated"] = True
        d = p.dma("sync", lambda s, osl=osl, c=c, par=par: s.dma_start(
            out=tbuf[par][:], in_=WDQ[osl * 512:(osl + 1) * 512,
                                      c * 128:(c + 1) * 128],
            transpose=True), DWT[par], wait=w)
        p.dma("sync", lambda s, osl=osl, c=c, par=par: s.dma_start(
            out=dram["WDQT"][c * 128:(c + 1) * 128, osl * 512:(osl + 1) * 512],
            in_=tbuf[par][:]), DWTS[par], wait=[(DWT[par], d)])
        return True

    return emit_group, emit_wdqt, DQI + DQO + DWT + DWTS


def _phase_att0(p, sb, psa, dram, T, kT, vtok, DCC, ones_col,
                ones128, ident_b, slopes, interleave=None):
    SP, SV, SS = p.esem["tensor"], p.esem["vector"], p.esem["scalar"]
    QT, ATN = dram["QT"], dram["ATN"]
    nkt = T // 128
    nqb = (T + 511) // 512
    DQT = [p.sem(f"C_DQT{i}") for i in range(2)]
    DAS = [p.sem(f"C_DAS{i}") for i in range(2)]
    DKN = p.sem("C_DKN")
    DKN2 = p.sem("C_DKN2")

    qbuf = [sb(f"Cq{i}", [128, T], BF16) for i in range(2)]
    off = []
    o = 0
    for kt in range(nkt):
        off.append(o)
        o += T - kt * 128
    ptil = [sb(f"Cptil{i}", [128, o], BF16) for i in range(2)]
    vt_h = [sb(f"Cvth{i}", [128, T], BF16) for i in range(2)]
    rel_t = sb("Crel", [128, 2048], F32)
    sbR = [sb(f"CsbR{i}", [128, 1024], F32) for i in range(2)]
    sqb = sb("Csqb", [128, T], BF16)
    knrow = sb("Cknrow", [1, 2048], F32)
    knrb = sb("Cknrb", [1, 2048], BF16)
    kncol = sb("Ckncol", [128, 16], BF16, is_transpose_dest=True)
    kncolf = sb("Ckncolf", [128, 16], F32)
    knm = sb("Cknm", [1, 4], F32)
    hsc = [sb(f"Chsc{i}", [128, 24], F32) for i in range(2)]
    fcol = [sb(f"Cfcol{i}", [128, 40], F32) for i in range(2)]
    fbf = [sb(f"Cfbf{i}", [128, 16], BF16) for i in range(2)]
    sm1 = sb("Csm1", [1, 8], F32)
    sm1b = sb("Csm1b", [1, 8], BF16)
    atn_st = [sb(f"Catn{i}", [128, T], BF16) for i in range(2)]
    lf = sb("Clf", [1, 1024], F32)
    rlb = sb("Crlb", [1, 512], BF16)
    rb = sb("Crb", [128, 512], F32)
    psS = [psa(f"CpsS{i}", [128, 1024]) for i in range(2)]
    psAV = psa("CpsAV", [128, 512])
    psL = psa("CpsL", [128, 512])
    psR = psa("CpsR", [128, 512])
    psQ = psa("CpsQ", [128, 512])

    p.dma("gpsimd", lambda g: g.dma_start(out=rel_t[:], in_=dram["rel"][:, :]), DCC)
    d_cc = DCC.cur
    p.wait("vector", DCC, d_cc)
    p.wait("tensor", DCC, d_cc)
    p.wait("scalar", DCC, d_cc)
    p.dma("sync", lambda s: s.dma_start(out=qbuf[0][:], in_=QT[0:128, :]), DQT[0])

    sk = p.step("scalar", lambda sc: sc.activation(sqb[:], kT[:], AF.Square))
    for j in range(nqb):
        bs = min(512, T - j * 512)
        mm = p.step("tensor", lambda t, j=j, bs=bs: t.matmul(
            psQ[0:1, 0:bs], ones128[:], sqb[:, j * 512:j * 512 + bs],
            start=True, stop=True), wait=[(SS, sk)] if j == 0 else [(SV, SV.cur)])
        p.step("vector", lambda v, j=j, bs=bs: v.tensor_copy(
            knrow[0:1, j * 512:j * 512 + bs], psQ[0:1, 0:bs]), wait=[(SP, mm)])
    p.step("vector", lambda v: v.reduce_max(knm[0:1, 0:1], knrow[0:1, 0:T], axis=AX))
    p.step("vector", lambda v: v.memset(knrb[0:1, :], 0.0))
    kcp = p.step("vector", lambda v: v.tensor_copy(knrb[0:1, 0:T], knrow[0:1, 0:T]))
    d_kn = p.dma("sync", lambda s: s.dma_start(
        out=dram["vscr"][:, :].rearrange("h d -> (h d)")[None, 0:2048],
        in_=knrb[0:1, 0:2048]), DKN, wait=[(SV, kcp)])
    d_kn2 = p.dma("sync", lambda s: s.dma_start(
        out=kncol[:, 0:16],
        in_=dram["vscr"][:, :].rearrange("h d -> (h d)")[0:2048].rearrange(
            "(a b) -> a b", b=128), transpose=True), DKN2, wait=[(DKN, d_kn)])
    p.step("vector", lambda v, nkt=nkt: v.tensor_copy(kncolf[:, 0:nkt],
                                                      kncol[:, 0:nkt]),
           wait=[(DKN2, d_kn2)])

    slab_exp = [0, 0]
    slab_stt = [0, 0]
    half_ctr = 0
    st = {}
    av_mult_ms = [0]

    def emit_prep(h):
        hp = h % 2
        d_q = DQT[hp].cur
        sq = p.step("scalar", lambda sc, hp=hp: sc.activation(
            sqb[:], qbuf[hp][:], AF.Square),
            wait=[(DQT[hp], d_q), (SV, SV.cur), (SP, SP.cur)])
        for j in range(nqb):
            bs = min(512, T - j * 512)
            mm = p.step("tensor", lambda t, j=j, bs=bs: t.matmul(
                psQ[0:1, 0:bs], ones128[:], sqb[:, j * 512:j * 512 + bs],
                start=True, stop=True),
                wait=[(SS, sq)] if j == 0 else [(SV, SV.cur)])
            p.step("vector", lambda v, j=j, bs=bs: v.reduce_max(
                sm1[0:1, j:j + 1], psQ[0:1, 0:bs], axis=AX), wait=[(SP, mm)])
        for j in range(1, nqb):
            p.step("vector", lambda v, j=j: v.tensor_tensor(
                sm1[0:1, 0:1], sm1[0:1, 0:1], sm1[0:1, j:j + 1], OP.max))
        p.step("vector", lambda v: v.tensor_mul(sm1[0:1, 4:5], sm1[0:1, 0:1],
                                                knm[0:1, 0:1]))
        ms = p.step("scalar", lambda sc: sc.activation(
            sm1[0:1, 5:6], sm1[0:1, 4:5], AF.Sqrt), wait=[(SV, SV.cur)])
        cb = p.step("vector", lambda v: (
            v.tensor_copy(sm1b[0:1, 0:1], sm1[0:1, 0:1]),
            v.tensor_copy(sm1b[0:1, 5:6], sm1[0:1, 5:6]))[-1], wait=[(SS, ms)])
        bc1 = p.step("tensor", lambda t: t.matmul(
            psQ[:, 0:1], ones_col[:], sm1b[0:1, 0:1],
            start=True, stop=True), wait=[(SV, cb)])
        bc2 = p.step("tensor", lambda t: t.matmul(
            psQ[:, 4:5], ones_col[:], sm1b[0:1, 5:6],
            start=True, stop=True))
        p.step("vector", lambda v, hp=hp: v.tensor_copy(hsc[hp][:, 0:1],
                                                        psQ[:, 0:1]),
               wait=[(SP, bc2)])
        p.step("vector", lambda v, hp=hp: v.tensor_copy(hsc[hp][:, 1:2],
                                                        psQ[:, 4:5]))
        p.step("vector", lambda v, hp=hp, nkt=nkt: v.tensor_tensor(
            fcol[hp][:, 0:nkt], kncolf[:, 0:nkt],
            hsc[hp][:, 0:1].broadcast_to((128, nkt)), OP.mult))
        sq2 = p.step("scalar", lambda sc, hp=hp, nkt=nkt: sc.activation(
            hsc[hp][:, 8:8 + nkt], fcol[hp][:, 0:nkt], AF.Sqrt),
            wait=[(SV, SV.cur)])
        bn = p.step("vector", lambda v, hp=hp, nkt=nkt: v.tensor_scalar(
            hsc[hp][:, 8:8 + nkt], hsc[hp][:, 8:8 + nkt], -1.0, None, OP.mult),
            wait=[(SS, sq2)])
        p.step("vector", lambda v, hp=hp, nkt=nkt: v.tensor_scalar(
            fcol[hp][:, 20:20 + nkt], hsc[hp][:, 8:8 + nkt], -1.0,
            hsc[hp][:, 1:2], OP.mult, OP.subtract))
        fe = p.step("scalar", lambda sc, hp=hp, nkt=nkt: sc.activation(
            fcol[hp][:, 20:20 + nkt], fcol[hp][:, 20:20 + nkt], AF.Exp),
            wait=[(SV, SV.cur)])
        p.step("vector", lambda v, hp=hp, nkt=nkt: v.tensor_copy(
            fbf[hp][:, 0:nkt], fcol[hp][:, 20:20 + nkt]), wait=[(SS, fe)])
        for kt in range(nkt):
            p.step("vector", lambda v, kt=kt, hp=hp: v.tensor_scalar(
                vt_h[hp][:, kt * 128:(kt + 1) * 128],
                vtok[:, kt * 128:(kt + 1) * 128],
                fcol[hp][:, 20 + kt:21 + kt], None, OP.mult))
        st[h] = {"bneg": SV.cur, "vprep": SV.cur, "bn_ss": SS.cur}

    def emit_pass1(h, interleave=None):
        nonlocal half_ctr
        hp = h % 2
        d_q = DQT[hp].cur
        for kt in range(nkt):
            L = T - kt * 128
            nhalf = (L + 1023) // 1024
            for h2 in range(nhalf):
                sl = half_ctr % 2
                half_ctr += 1
                hl = min(1024, L - h2 * 1024)
                nj = (hl + 511) // 512
                for j in range(nj):
                    n = min(512, hl - j * 512)
                    qp0 = h2 * 1024 + j * 512
                    qg = kt * 128 + qp0
                    p.step("tensor", lambda t, kt=kt, sl=sl, n=n, qg=qg, qp0=qp0,
                           h2=h2, hp=hp: t.matmul(
                        psS[sl][:, qp0 - h2 * 1024: qp0 - h2 * 1024 + n],
                        kT[:, kt * 128:(kt + 1) * 128],
                        qbuf[hp][:, qg:qg + n], start=True, stop=True),
                        wait=([(DQT[hp], d_q), (SV, slab_stt[sl])]
                              if j == 0 else []))
                mm = SP.cur
                p.step("vector", lambda v, sl=sl, hl=hl, h2=h2, h=h: v.scalar_tensor_tensor(
                    sbR[sl][:, 0:hl], rel_t[:, h2 * 1024:h2 * 1024 + hl],
                    float(slopes[h]), psS[sl][:, 0:hl], OP.mult, OP.add),
                    wait=[(SP, mm), (SS, slab_exp[sl])])
                slab_stt[sl] = SV.cur
                p.step("scalar", lambda sc, kt=kt, sl=sl, hl=hl, h2=h2, hp=hp: sc.activation(
                    ptil[hp][:, off[kt] + h2 * 1024: off[kt] + h2 * 1024 + hl],
                    sbR[sl][:, 0:hl], AF.Exp, bias=hsc[hp][:, 8 + kt:9 + kt]),
                    wait=([(SV, max(slab_stt[sl], st[h]["bneg"]))]
                          if (kt == 0 and h2 == 0) else [(SV, slab_stt[sl])]))
                slab_exp[sl] = SS.cur
            if interleave is not None:
                interleave(kt)
        st[h]["p1_pe"] = SP.cur
        st[h]["exp_all"] = SS.cur

    def emit_av_mm(h, qb):
        hp = h % 2
        bs = min(512, T - qb * 512)
        kts = [kt for kt in range(nkt) if kt * 128 < qb * 512 + bs]
        first = True
        for kt in kts:
            lo = max(qb * 512, kt * 128)
            n = qb * 512 + bs - lo
            srcq = off[kt] + lo - kt * 128
            p.step("tensor", lambda t, kt=kt, n=n, lo=lo, srcq=srcq, qb=qb, hp=hp,
                   first=first, last=(kt == kts[-1]): t.matmul(
                psAV[:, lo - qb * 512: lo - qb * 512 + n],
                vt_h[hp][:, kt * 128:(kt + 1) * 128],
                ptil[hp][:, srcq:srcq + n],
                start=first, stop=last),
                wait=(([(SS, st[h]["exp_all"]),
                        (SV, max(st[h]["vprep"], av_mult_ms[0]))]
                       if qb == 0 else [(SV, av_mult_ms[0])]) if first else []))
            p.step("tensor", lambda t, kt=kt, n=n, lo=lo, srcq=srcq, qb=qb, hp=hp,
                   first=first, last=(kt == kts[-1]): t.matmul(
                psL[0:1, lo - qb * 512: lo - qb * 512 + n],
                fbf[hp][:, kt:kt + 1],
                ptil[hp][:, srcq:srcq + n],
                start=first, stop=last))
            first = False
        av_ms = SP.cur
        p.step("scalar", lambda sc, bs=bs: sc.activation(
            lf[0:1, 0:bs], psL[0:1, 0:bs], AF.Copy), wait=[(SP, av_ms)])
        p.step("scalar", lambda sc, bs=bs: sc.activation(
            lf[0:1, 512:512 + bs], lf[0:1, 0:bs], AF.Ln))
        rcp = p.step("scalar", lambda sc, bs=bs: sc.activation(
            rlb[0:1, 0:bs], lf[0:1, 512:512 + bs], AF.Exp, scale=-1.0))
        return rcp

    def emit_av_fin(h, qb, rcp):
        hp = h % 2
        bs = min(512, T - qb * 512)
        rbm = p.step("tensor", lambda t, bs=bs: t.matmul(
            psR[:, 0:bs], ones_col[:],
            rlb[0:1, 0:bs], start=True, stop=True), wait=[(SS, rcp)])
        rbc = p.step("scalar", lambda sc, bs=bs: sc.activation(
            rb[:, 0:bs], psR[:, 0:bs], AF.Copy), wait=[(SP, rbm)])
        p.step("vector", lambda v, qb=qb, bs=bs, hp=hp: v.tensor_mul(
            atn_st[hp][:, qb * 512:qb * 512 + bs], psAV[:, 0:bs], rb[:, 0:bs]),
            wait=([(DAS[hp], DAS[hp].cur), (SS, rbc)] if qb == 0 else [(SS, rbc)]))
        av_mult_ms[0] = SV.cur
        if qb == nqb - 1:
            p.dma("sync", lambda s, h=h, hp=hp: s.dma_start(
                out=ATN[h * 128:(h + 1) * 128, :], in_=atn_st[hp][:]),
                DAS[hp], wait=[(SV, SV.cur)])

    actions = {"q": []}

    def drain_actions():
        while actions["q"]:
            kind, h_, qb_, rcp_ = actions["q"].pop(0)
            if kind == "mm":
                rcp = emit_av_mm(h_, qb_)
                emit_av_fin(h_, qb_, rcp)
            else:
                emit_av_fin(h_, qb_, rcp_)

    def pump(kt):
        if actions["q"] and kt % 2 == 0:
            kind, h_, qb_, rcp_ = actions["q"].pop(0)
            if kind == "mm":
                rcp = emit_av_mm(h_, qb_)
                actions["q"].insert(0, ("fin", h_, qb_, rcp))
            else:
                emit_av_fin(h_, qb_, rcp_)

    for h in range(32):
        if h + 1 < 32:
            gate = st[h - 1]["p1_pe"] if h >= 1 else 0
            p.dma("sync", lambda s, h=h: s.dma_start(
                out=qbuf[(h + 1) % 2][:], in_=QT[(h + 1) * 128:(h + 2) * 128, :]),
                DQT[(h + 1) % 2], wait=[(SP, gate)] if gate else [])
        emit_prep(h)
        if interleave is not None:
            interleave()
        emit_pass1(h, interleave=pump)
        drain_actions()
        actions["q"] = [("mm", h, qb, None) for qb in range(nqb)]
    drain_actions()
    return [DQT[0], DQT[1], DAS[0], DAS[1], DKN, DKN2]


def _phase_gate(p, sb, psa, dram, T, gwTb, aTb, ident_b, p8b, hT):
    SP, SV, SS = p.esem["tensor"], p.esem["vector"], p.esem["scalar"]
    XN1 = dram["XN1"]
    NB = (T + 511) // 512
    DTR = p.sem("E_DTR")
    xnTb = sb("ExnT", [128, NCH * 512], BF16, is_transpose_dest=True)
    lg = sb("Elg", [128, 8], F32)
    est = sb("Eest", [128, 24], F32)
    ind = sb("Eind", [128, 8], F32)
    vsc = sb("Evsc", [128, 8], BF16)
    h_bf = sb("Ehbf", [128, 64], BF16)
    hT_sb = sb("EhTsb", [64, 128], BF16)
    vT_sb = sb("EvTsb", [8, 128], BF16)
    w2_sb = sb("Ew2sb", [64, 128], BF16)
    psGL = psa("EpsGL", [128, 512])
    psH = psa("EpsH", [128, 512])
    psT1 = psa("EpsT1", [128, 128], BF16)
    psT2 = psa("EpsT2", [128, 128], BF16)
    psW = psa("EpsW", [128, 128])

    pe_blk = 0
    for blk in range(NB):
        bs = min(512, T - blk * 512)
        ntt = bs // 128
        d_tr = DTR.cur
        for c in range(NCH):
            d_tr = p.dma("sync", lambda s, c=c, blk=blk, bs=bs: s.dma_start(
                out=xnTb[:, c * 512:c * 512 + bs],
                in_=XN1[blk * 512:blk * 512 + bs, c * 128:(c + 1) * 128],
                transpose=True), DTR, wait=[(SP, pe_blk)] if c == 0 else [])
        for tt in range(ntt):
            for c in range(NCH):
                p.step("tensor", lambda t, c=c, tt=tt: t.matmul(
                    psGL[:, 0:8], xnTb[:, c * 512 + tt * 128:c * 512 + (tt + 1) * 128],
                    gwTb[:, c * 8:(c + 1) * 8], start=(c == 0), stop=(c == NCH - 1)),
                    wait=[(DTR, d_tr), (SV, SV.cur)] if c == 0 else [])
            for c in range(NCH):
                p.step("tensor", lambda t, c=c, tt=tt: t.matmul(
                    psH[:, 0:64], xnTb[:, c * 512 + tt * 128:c * 512 + (tt + 1) * 128],
                    aTb[:, c * 64:(c + 1) * 64], start=(c == 0), stop=(c == NCH - 1)),
                    wait=[(SV, SV.cur)] if c == 0 else [])
            mm_ms = SP.cur
            p.step("vector", lambda v: v.tensor_copy(lg[:], psGL[:, 0:8]),
                   wait=[(SP, mm_ms)])
            p.step("vector", lambda v: v.reduce_max(est[:, 0:1], lg[:], axis=AX,
                                                    negate=True))
            p.step("vector", lambda v: v.tensor_scalar(
                est[:, 1:2], est[:, 0:1], -1.0, None, OP.mult))
            p.step("vector", lambda v: v.tensor_scalar(
                ind[:], lg[:], est[:, 1:2], None, OP.is_ge))
            p.step("vector", lambda v: v.scalar_tensor_tensor(
                est[:, 8:16], ind[:], -1e30, lg[:], OP.mult, OP.add))
            p.step("vector", lambda v: v.reduce_max(est[:, 2:3], est[:, 8:16], axis=AX))
            p.step("vector", lambda v: v.tensor_scalar(
                ind[:], lg[:], est[:, 2:3], None, OP.is_ge))
            e1 = p.step("scalar", lambda sc: sc.activation(
                est[:, 16:24], lg[:], AF.Exp, bias=est[:, 0:1]), wait=[(SV, SV.cur)])
            e2 = p.step("scalar", lambda sc: sc.activation(
                est[:, 3:4], est[:, 2:3], AF.Exp, bias=est[:, 0:1]))
            p.step("vector", lambda v: v.tensor_scalar(
                est[:, 4:5], est[:, 3:4], 1.0, None, OP.add), wait=[(SS, e2)])
            p.step("vector", lambda v: v.reciprocal(est[:, 5:6], est[:, 4:5]))
            p.step("vector", lambda v: v.tensor_mul(est[:, 16:24], est[:, 16:24],
                                                    ind[:]), wait=[(SS, e1)])
            vbms = p.step("vector", lambda v: v.tensor_scalar(
                vsc[:], est[:, 16:24], est[:, 5:6], 2.0, OP.mult, OP.mult))
            hbms = p.step("vector", lambda v: v.tensor_copy(h_bf[:], psH[:, 0:64]))
            t1 = p.step("tensor", lambda t: t.transpose(
                psT1[0:64, 0:128], h_bf[:], ident_b[:]), wait=[(SV, hbms)])
            t2 = p.step("tensor", lambda t: t.transpose(
                psT2[0:8, 0:128], vsc[:], ident_b[:]), wait=[(SV, vbms)])
            ev1 = p.step("vector", lambda v: v.tensor_copy(hT_sb[:], psT1[0:64, 0:128]),
                         wait=[(SP, t1)])
            ev2 = p.step("vector", lambda v: v.tensor_copy(vT_sb[:], psT2[0:8, 0:128]),
                         wait=[(SP, t2)])
            t3 = p.step("tensor", lambda t: t.matmul(
                psW[0:64, 0:128], p8b[:], vT_sb[:], start=True, stop=True),
                wait=[(SV, ev2)])
            ev3 = p.step("vector", lambda v: v.tensor_copy(w2_sb[:], psW[0:64, 0:128]),
                         wait=[(SP, t3)])
            p.step("vector", lambda v, blk=blk, tt=tt: v.tensor_mul(
                hT[:, blk * 512 + tt * 128: blk * 512 + (tt + 1) * 128],
                hT_sb[:], w2_sb[:]))
        pe_blk = SP.cur
    return [DTR]


def _phase_att1(p, sb, psa, dram, T, c_col, vb_b, maskT_b):
    SP, SV, SS = p.esem["tensor"], p.esem["vector"], p.esem["scalar"]
    XN1, S1 = dram["XN1"], dram["S1"]
    nkt = T // 128
    DVH = p.sem("F_DVH")
    DTR = p.sem("F_DTR")
    DS1 = [p.sem(f"F_DS1{i}") for i in range(2)]
    vh = sb("Fvh", [128, (T // 128) * 1024], BF16)
    xnTh = sb("FxnTh", [128, 8 * T], BF16, is_transpose_dest=True)
    qp = sb("Fqp", [128, 8 * T], BF16)
    off = []
    o = 0
    for kt in range(nkt):
        off.append(o)
        o += T - kt * 128
    attT = [sb(f"FattT{i}", [128, o], BF16) for i in range(2)]
    s1t = [sb(f"Fs1t{i}", [128, 1024], F32) for i in range(2)]
    psS = [psa(f"FpsS{i}", [128, 512]) for i in range(2)]
    psAV = [psa(f"FpsAV{i}", [128, 512]) for i in range(2)]

    st = {}
    av_done_pe = [0]
    av_units = {"list": []}

    def emit_av_unit(h, tt, dsl):
        hp = h % 2
        b = (tt * 2 + dsl) % 2
        for stt in range(tt + 1):
            srcq = off[stt] + tt * 128 - stt * 128
            p.step("tensor", lambda t, stt=stt, tt=tt, dsl=dsl, b=b, srcq=srcq,
                   hp=hp, h=h: t.matmul(
                psAV[b][:], attT[hp][:, srcq:srcq + 128],
                vh[:, stt * 1024 + dsl * 512: stt * 1024 + dsl * 512 + 512],
                start=(stt == 0), stop=(stt == tt)),
                wait=([(SS, st[h]["sig_all"]), (SV, SV.cur),
                       (DVH, st[h]["d_vh"])] if stt == 0 else []))
        av_ms = SP.cur
        sb_par = (tt * 2 + dsl) % 2
        ev = p.step("vector", lambda v, h=h, tt=tt, dsl=dsl, b=b, sb_par=sb_par: v.tensor_mul(
            s1t[sb_par][:, 0:512], psAV[b][:],
            vb_b[:, (h % 4) * 1024 + dsl * 512: (h % 4) * 1024 + dsl * 512 + 512]),
            wait=[(SP, av_ms), (DS1[sb_par], DS1[sb_par].cur)])
        p.dma("sync", lambda s, h=h, tt=tt, dsl=dsl, sb_par=sb_par: s.dma_start(
            out=S1[tt * 128:(tt + 1) * 128,
                   (h % 4) * 1024 + dsl * 512: (h % 4) * 1024 + dsl * 512 + 512],
            in_=s1t[sb_par][:, 0:512]), DS1[sb_par], wait=[(SV, ev)])
        av_done_pe[0] = SP.cur

    def pump_av(n):
        for _ in range(n):
            if av_units["list"]:
                h, tt, dsl = av_units["list"].pop(0)
                emit_av_unit(h, tt, dsl)

    for h in range(4):
        hp = h % 2
        pe_prev = st[h - 1]["s_pe"] if h >= 1 else 0
        d_tr = DTR.cur
        for c in range(8):
            d_tr = p.dma("sync", lambda s, c=c, h=h: s.dma_start(
                out=xnTh[:, c * T:(c + 1) * T],
                in_=XN1[:, h * 1024 + c * 128: h * 1024 + (c + 1) * 128],
                transpose=True), DTR,
                wait=[(SP, pe_prev)] if c == 0 else [])
        for c in range(8):
            p.step("vector", lambda v, c=c, h=h: v.tensor_scalar(
                qp[:, c * T:(c + 1) * T], xnTh[:, c * T:(c + 1) * T],
                c_col[:, h * 8 + c:h * 8 + c + 1], None, OP.mult),
                wait=[(DTR, d_tr)] if c == 0 else [])
        qp_ms = SV.cur
        st[h] = {"d_vh": 0}
        for kt in range(nkt):
            L = T - kt * 128
            nj = (L + 511) // 512
            for j in range(nj):
                jb = (kt * 7 + j) % 2
                n = min(512, L - j * 512)
                q0 = kt * 128 + j * 512
                for c in range(8):
                    p.step("tensor", lambda t, c=c, kt=kt, jb=jb, n=n, q0=q0: t.matmul(
                        psS[jb][:, 0:n], xnTh[:, c * T + kt * 128:c * T + (kt + 1) * 128],
                        qp[:, c * T + q0:c * T + q0 + n],
                        start=(c == 0), stop=(c == 7)),
                        wait=[(SV, qp_ms), (SS, SS.cur)] if c == 0 else [])
                ms = SP.cur
                p.step("scalar", lambda sc, kt=kt, j=j, jb=jb, n=n, hp=hp: sc.activation(
                    attT[hp][:, off[kt] + j * 512: off[kt] + j * 512 + n],
                    psS[jb][:, 0:n], AF.Sigmoid), wait=[(SP, ms)])
            p.step("vector", lambda v, kt=kt, hp=hp: v.tensor_mul(
                attT[hp][:, off[kt]:off[kt] + 128], attT[hp][:, off[kt]:off[kt] + 128],
                maskT_b[:]), wait=[(SS, SS.cur)])
            pump_av(2)
        pump_av(64)
        st[h]["s_pe"] = SP.cur
        st[h]["sig_all"] = SS.cur
        st[h]["mask_all"] = SV.cur
        st[h]["d_vh"] = p.dma("gpsimd", lambda g, h=h: g.dma_start(
            out=vh[:].rearrange("p (s d) -> p s d", d=1024),
            in_=XN1[:, h * 1024:(h + 1) * 1024].rearrange("(s p) d -> p s d", p=128)),
            DVH, wait=[(SP, av_done_pe[0])])
        av_units["list"] = [(h, tt, dsl) for tt in range(nkt) for dsl in range(2)]
    pump_av(64)
    return [DVH, DTR, DS1[0], DS1[1]]



def _mk_epi_store_bf16(p, sb, dram, T, tag, out_name, scale=None):
    SS = p.esem["scalar"]
    obuf = [sb(f"{tag}ob{i}", [128, 512], BF16) for i in range(2)]
    DST = [p.sem(f"{tag}DO{i}") for i in range(2)]
    OUT = dram[out_name]
    TH = T // 2

    def epi(p, th, os, tt, ps_ap, pe_ms):
        par = (os * (TH // 128) + tt) % 2
        kw = {} if scale is None else {"scale": np.float32(scale)}
        ev = p.step("scalar", lambda sc, par=par: sc.activation(
            obuf[par][:], ps_ap, AF.Copy, **kw),
            wait=[(p.esem["tensor"], pe_ms), (DST[par], DST[par].cur)])
        row = th * TH + tt * 128
        p.dma("scalar", lambda s, row=row, os=os, par=par: s.dma_start(
            out=OUT[row:row + 128, os * 512:(os + 1) * 512], in_=obuf[par][:]),
            DST[par], wait=[(SS, ev)])
        return (SS, ev)
    return epi, [DST[0], DST[1]]


def _mk_epi_silu_store(p, sb, dram, T, tag, out_name):
    SS = p.esem["scalar"]
    obuf = [sb(f"{tag}ob{i}", [128, 512], F32) for i in range(2)]
    DST = [p.sem(f"{tag}DO{i}") for i in range(2)]
    OUT = dram[out_name]
    TH = T // 2

    def epi(p, th, os, tt, ps_ap, pe_ms):
        par = (os * (TH // 128) + tt) % 2
        ev = p.step("scalar", lambda sc, par=par: sc.activation(
            obuf[par][:], ps_ap, AF.Silu),
            wait=[(p.esem["tensor"], pe_ms), (DST[par], DST[par].cur)])
        row = th * TH + tt * 128
        p.dma("scalar", lambda s, row=row, os=os, par=par: s.dma_start(
            out=OUT[row:row + 128, os * 512:(os + 1) * 512], in_=obuf[par][:]),
            DST[par], wait=[(SS, ev)])
        return (SS, ev)
    return epi, [DST[0], DST[1]]


def _mk_epi_gate(p, sb, dram, T, tag, g_name, x_name, out_name, silu=False):
    SS, SV = p.esem["scalar"], p.esem["vector"]
    sbuf = [sb(f"{tag}sb{i}", [128, 512], F32) for i in range(2)]
    gbuf = [sb(f"{tag}gb{i}", [128, 512], F32) for i in range(2)]
    xbuf = [sb(f"{tag}xb{i}", [128, 512], F32) for i in range(2)]
    DGX = [p.sem(f"{tag}DG{i}") for i in range(2)]
    DST = [p.sem(f"{tag}DO{i}") for i in range(2)]
    GA, XA, OUT = dram[g_name], dram[x_name], dram[out_name]
    TH = T // 2
    ld_gate = [0, 0]

    def epi(p, th, os, tt, ps_ap, pe_ms):
        par = (os * (TH // 128) + tt) % 2
        row = th * TH + tt * 128
        w = [(SV, ld_gate[par])] if ld_gate[par] else []
        d_g = p.dma("sync", lambda g, row=row, os=os, par=par: g.dma_start(
            out=gbuf[par][:], in_=GA[row:row + 128, os * 512:(os + 1) * 512]),
            DGX[par], wait=w)
        d_g = p.dma("sync", lambda g, row=row, os=os, par=par: g.dma_start(
            out=xbuf[par][:], in_=XA[row:row + 128, os * 512:(os + 1) * 512]),
            DGX[par])
        if silu:
            ev = p.step("scalar", lambda sc, par=par: sc.activation(
                sbuf[par][:], ps_ap, AF.Silu),
                wait=[(p.esem["tensor"], pe_ms), (DST[par], DST[par].cur)])
        else:
            ev = p.step("scalar", lambda sc, par=par: sc.activation(
                sbuf[par][:], ps_ap, AF.Copy),
                wait=[(p.esem["tensor"], pe_ms), (DST[par], DST[par].cur)])
        m1 = p.step("vector", lambda v, par=par: v.tensor_mul(
            sbuf[par][:], sbuf[par][:], gbuf[par][:]),
            wait=[(SS, ev), (DGX[par], d_g)])
        m2 = p.step("vector", lambda v, par=par: v.tensor_add(
            sbuf[par][:], sbuf[par][:], xbuf[par][:]))
        ld_gate[par] = m2
        p.dma("sync", lambda s, row=row, os=os, par=par: s.dma_start(
            out=OUT[row:row + 128, os * 512:(os + 1) * 512], in_=sbuf[par][:]),
            DST[par], wait=[(SV, m2)])
        return (SV, m2)
    return epi, [DGX[0], DGX[1], DST[0], DST[1]]



def build_program(T=2048, phases=("PRO", "A", "B0", "B1", "B2", "C", "D",
                                  "E", "F", "G"),
                  debug_outputs=False):
    nT = T // 128
    nc = bass.Bass("TRN2", target_bir_lowering=False)
    dram = {}

    def din(name, shape, dtype):
        dram[name] = nc.dram_tensor(name, shape, dtype, kind="ExternalInput").ap()

    def dscr(name, shape, dtype):
        kind = "ExternalOutput" if debug_outputs else "Internal"
        dram[name] = nc.dram_tensor(name, shape, dtype, kind=kind).ap()

    def dout(name, shape, dtype):
        dram[name] = nc.dram_tensor(name, shape, dtype, kind="ExternalOutput").ap()

    din("x0", [T, D], F32)
    din("wq", [D, D], F32)
    din("wkv", [D, 256], F32)
    din("wo", [D, D], F32)
    din("wqq", [D, D], I32)
    din("sT", [D, 32], F32)
    din("zT", [D, 32], F32)
    din("wbT", [D, D], F32)
    din("gwT", [D, 8], F32)
    din("aT", [D, 64], F32)
    din("bF", [64, D], F32)
    din("bvq", [4, 1024], F32)
    din("bvk", [4, 1024], F32)
    din("bvv", [4, 1024], F32)
    din("iota", [128, 16], F32)
    din("rel", [128, 2048], F32)
    din("prow4", [4, 128], F32)
    din("krow4", [4, 2048], F32)
    din("cmaskT", [128, 128], F32)
    din("maskT", [128, 128], F32)
    din("ident", [128, 128], F32)
    din("p8", [8, 64], F32)

    dscr("XN", [T, D], BF16)
    dscr("QT", [D, T], BF16)
    dscr("WDQ", [D, D], BF16)
    dscr("WDQT", [D, D], BF16)
    dscr("G", [T, D], F32)
    dscr("ATN", [D, T], BF16)
    dscr("X1", [T, D], F32)
    dscr("XN1", [T, D], BF16)
    dscr("S1", [T, D], F32)
    dscr("cscr", [4, 1024], BF16)
    dscr("vscr", [4, 1024], BF16)
    dout("out", [T, D], F32)
    if debug_outputs:
        dout("dbg_kT", [128, T], F32)
        dout("dbg_vtok", [128, T], F32)
        dout("dbg_ccol", [128, 32], F32)
        dout("dbg_vbb", [128, D], F32)
        dout("dbg_hT", [64, T], F32)

    slopes = _alibi_slopes(32)

    with contextlib.ExitStack() as ctx:
        def sb_top(nm, shp, dt, **kw):
            return ctx.enter_context(nc.sbuf_tensor(nm, shp, dt, **kw))

        ones_col = sb_top("ones_col", [1, 128], BF16)
        ones128 = sb_top("ones128", [128, 1], BF16)
        iota_t = sb_top("iota_t", [128, 16], F32)
        maskT_b = sb_top("maskT_b", [128, 128], BF16)
        ident_b = sb_top("ident_b", [128, 128], BF16)
        kT = sb_top("kT_res", [128, T], BF16)
        vtok = sb_top("vtok_res", [128, T], BF16)
        c_col = sb_top("c_col", [128, 32], BF16)
        c_colf = sb_top("c_colf", [128, 32], F32)
        vb_b = sb_top("vb_b", [128, D], F32)
        hT = sb_top("hT_res", [64, T], BF16)

        with nc.Block() as block:
            p = Prog(nc, ctx)
            SP, SV, SS = p.esem["tensor"], p.esem["vector"], p.esem["scalar"]

            DC = p.sem("DC")
            p.step("vector", lambda v: (
                v.memset(ones_col[:], 1.0), v.memset(ones128[:], 1.0))[-1])
            p.dma("gpsimd", lambda g: g.dma_start(out=iota_t[:], in_=dram["iota"][:, :]), DC)
            p.dma("gpsimd", lambda g: g.dma_start(out=maskT_b[:], in_=dram["maskT"][:, :]), DC)
            p.dma("gpsimd", lambda g: g.dma_start(out=ident_b[:], in_=dram["ident"][:, :]), DC)
            d_consts = DC.cur

            if "PRO" in phases:
                proctx = contextlib.ExitStack()
                sb_pro = lambda nm, shp, dt, **kw: proctx.enter_context(
                    nc.sbuf_tensor(nm, shp, dt, **kw))
                bvt = {k: sb_pro(f"bv_{k}", [4, 1024], F32) for k in ("q", "k", "v")}
                bst = sb_pro("bv_st", [4, 8], F32)
                hrd = {k: sb_pro(f"hrd_{k}", [4, 1024], F32) for k in ("q", "k", "v")}
                vfl = sb_pro("vfl", [1, D], BF16)
                crow = sb_pro("crow", [4, 1024], BF16)
                DBV = p.sem("DBV")
                DCB = p.sem("DCB")
                DCB2 = p.sem("DCB2")
                for i, k in enumerate(("q", "k", "v")):
                    p.dma("gpsimd", lambda g, k=k: g.dma_start(
                        out=bvt[k][:], in_=dram["bv" + k][:, :]), DBV)
                d_bv = DBV.cur
                for i, k in enumerate(("q", "k", "v")):
                    p.step("vector", lambda v, k=k, i=i: v.reduce_sum(
                        bst[:, i:i + 1], bvt[k][:], axis=AX,
                        apply_absolute_value=True), wait=[(DBV, d_bv)])
                    p.step("vector", lambda v, k=k, i=i: v.tensor_scalar(
                        bst[:, 4 + i:5 + i], bst[:, i:i + 1], 1.0 / 1024, None, OP.mult))
                    sg1 = p.step("scalar", lambda sc, k=k: sc.activation(
                        hrd[k][:], bvt[k][:], AF.Sign), wait=[(DBV, d_bv)])
                    p.step("vector", lambda v, k=k: v.scalar_tensor_tensor(
                        bvt[k][:], hrd[k][:], -1.0, hrd[k][:], OP.abs_max, OP.add),
                        wait=[(SS, sg1)]) if False else None
                    ab1 = p.step("scalar", lambda sc, k=k: sc.activation(
                        bvt[k][:], hrd[k][:], AF.Abs), wait=[(SV, SV.cur)])
                    p.step("vector", lambda v, k=k: v.tensor_sub(
                        hrd[k][:], hrd[k][:], bvt[k][:]), wait=[(SS, ab1)])
                    p.step("vector", lambda v, k=k: v.tensor_scalar(
                        hrd[k][:], hrd[k][:], 1.0, None, OP.add))
                    p.step("vector", lambda v, k=k, i=i: v.tensor_scalar(
                        hrd[k][:], hrd[k][:], bst[:, 4 + i:5 + i], None, OP.mult))
                p.step("vector", lambda v: v.tensor_mul(hrd["q"][:], hrd["q"][:],
                                                        hrd["k"][:]))
                cms = p.step("vector", lambda v: v.tensor_scalar(
                    crow[:], hrd["q"][:], 0.125, None, OP.mult))
                d_cs = p.dma("gpsimd", lambda g: g.dma_start(
                    out=dram["cscr"][:, :], in_=crow[:]), DCB, wait=[(SV, cms)])
                d_cc = p.dma("sync", lambda s: s.dma_start(
                    out=c_col[:], in_=dram["cscr"][:, :].rearrange(
                        "h (c p) -> (h c) p", p=128), transpose=True),
                    DCB2, wait=[(DCB, d_cs)])
                p.step("vector", lambda v: v.tensor_copy(c_colf[:], c_col[:]),
                       wait=[(DCB2, d_cc)])
                DVB = p.sem("DVB")
                vscr = dram["vscr"]
                d_vs = p.dma("gpsimd", lambda g: g.dma_start(
                    out=vscr[:, :], in_=hrd["v"][:]), DVB, wait=[(SV, SV.cur)])
                d_vf = p.dma("gpsimd", lambda g: g.dma_start(
                    out=vfl[:], in_=vscr[:, :].rearrange("h d -> (h d)")[None, :]),
                    DVB, wait=[(DVB, d_vs)])
                with nc.psum_tensor("psVB", [128, 512], F32) as psVB:
                    for j in range(8):
                        bm = p.step("tensor", lambda t, j=j, psVB=psVB: t.matmul(
                            psVB[:], ones_col[:], vfl[0:1, j * 512:(j + 1) * 512],
                            start=True, stop=True),
                            wait=[(DVB, d_vf), (SV, SV.cur)])
                        p.step("vector", lambda v, j=j, psVB=psVB: v.tensor_copy(
                            vb_b[:, j * 512:(j + 1) * 512], psVB[:]), wait=[(SP, bm)])
                proctx.close()
                p.barrier(dma_sems=[DC, DBV, DCB, DCB2, DVB])
            else:
                p.barrier(dma_sems=[DC])

            if "A" in phases:
                with contextlib.ExitStack() as pctx:
                    sbl = lambda nm, shp, dt, **kw: pctx.enter_context(
                        nc.sbuf_tensor(nm, shp, dt, **kw))
                    psl = lambda nm, shp, dt=F32: pctx.enter_context(
                        nc.psum_tensor(nm, shp, dt))
                    wkvb = sbl("wkvb", [128, NCH * 256], BF16)
                    DCA = p.sem("DCA")
                    p.dma("gpsimd", lambda g: g.dma_start(
                        out=wkvb[:].rearrange("p (k o) -> p k o", o=256),
                        in_=dram["wkv"][:, :].rearrange("(k p) o -> p k o", p=128)), DCA)
                    sems, ln_ms = _phase_ln(p, sbl, dram, "x0", "XN", T, "A")
                    p.wait("tensor", DCA, DCA.cur)
                    kvsems = _phase_kv(p, sbl, psl, dram, T, wkvb, kT, vtok,
                                       ln_ms)
                    p.barrier(dma_sems=sems + kvsems + [DCA])

            if "B1" in phases:
                with contextlib.ExitStack() as pctx:
                    sbl = lambda nm, shp, dt, **kw: pctx.enter_context(
                        nc.sbuf_tensor(nm, shp, dt, **kw))
                    psl = lambda nm, shp, dt=F32: pctx.enter_context(
                        nc.psum_tensor(nm, shp, dt))
                    b1sems = _phase_proj_fm(p, sbl, psl, dram, T, "B1", "XN",
                                            "wq", "QT", Q_SCALE)
                    p.barrier(dma_sems=b1sems)

            if "C" in phases:
                with contextlib.ExitStack() as pctx:
                    sbl = lambda nm, shp, dt, **kw: pctx.enter_context(
                        nc.sbuf_tensor(nm, shp, dt, **kw))
                    psl = lambda nm, shp, dt=F32: pctx.enter_context(
                        nc.psum_tensor(nm, shp, dt))
                    b0sems = []
                    b0_group = None
                    if "B0" in phases:
                        sTt = sbl("sTt", [128, NCH * 32], F32)
                        z8t = sbl("z8t", [128, NCH * 32], F32)
                        DC0 = p.sem("DC0")
                        p.dma("gpsimd", lambda g: g.dma_start(
                            out=sTt[:].rearrange("p (c g) -> p c g", g=32),
                            in_=dram["sT"][:, :].rearrange("(c p) g -> p c g", p=128)), DC0)
                        p.dma("gpsimd", lambda g: g.dma_start(
                            out=z8t[:].rearrange("p (c g) -> p c g", g=32),
                            in_=dram["zT"][:, :].rearrange("(c p) g -> p c g", p=128)), DC0)
                        p.step("vector", lambda v: v.scalar_tensor_tensor(
                            z8t[:], sTt[:], -8.0, z8t[:], OP.mult, OP.add),
                            wait=[(DC0, DC0.cur)])
                        b0_group, wdqt_group, b0sems = _phase_b0_maker(
                            p, sbl, psl, dram, sTt, z8t)
                        b0sems = b0sems + [DC0]

                        def c_inter(b0_group=b0_group):
                            for _ in range(8):
                                b0_group()
                        b0_pump = c_inter
                    DCC = p.sem("DCC")
                    sems = _phase_att0(p, sbl, psl, dram, T, kT, vtok, DCC,
                                       ones_col, ones128, ident_b, slopes,
                                       interleave=(b0_pump if b0_group is not None
                                                   else None))
                    if b0_group is not None:
                        while b0_group():
                            pass
                    sems = sems + [DCC] + b0sems
                    p.barrier(dma_sems=sems)

            if "B2" in phases:
                with contextlib.ExitStack() as pctx:
                    sbl = lambda nm, shp, dt, **kw: pctx.enter_context(
                        nc.sbuf_tensor(nm, shp, dt, **kw))
                    psl = lambda nm, shp, dt=F32: pctx.enter_context(
                        nc.psum_tensor(nm, shp, dt))
                    epi, esems = _mk_epi_silu_store(p, sbl, dram, T, "B2", "XN", "WDQ",
                                       "G") if False else _mk_epi_silu_store(
                        p, sbl, dram, T, "B2", "G")
                    sems = _phase_proj(p, sbl, psl, dram, T, "B2", "XN", "WDQ",
                                       "G", epi, transpose_weight=True)
                    p.barrier(dma_sems=sems + esems)

            if "D" in phases:
                with contextlib.ExitStack() as pctx:
                    sbl = lambda nm, shp, dt, **kw: pctx.enter_context(
                        nc.sbuf_tensor(nm, shp, dt, **kw))
                    psl = lambda nm, shp, dt=F32: pctx.enter_context(
                        nc.psum_tensor(nm, shp, dt))
                    epi, esems = _mk_epi_gate(p, sbl, dram, T, "D", "G", "x0", "X1")
                    sems = _phase_proj_atn(p, sbl, psl, dram, T, "D", "ATN",
                                           "wo", epi)
                    p.barrier(dma_sems=sems + esems)

            if "E" in phases:
                with contextlib.ExitStack() as pctx:
                    sbl = lambda nm, shp, dt, **kw: pctx.enter_context(
                        nc.sbuf_tensor(nm, shp, dt, **kw))
                    psl = lambda nm, shp, dt=F32: pctx.enter_context(
                        nc.psum_tensor(nm, shp, dt))
                    gwTb = sbl("gwTb", [128, NCH * 8], BF16)
                    aTb = sbl("aTb", [128, NCH * 64], BF16)
                    p8b = sbl("p8b", [8, 64], BF16)
                    DCE = p.sem("DCE")
                    p.dma("gpsimd", lambda g: g.dma_start(
                        out=gwTb[:].rearrange("p (c o) -> p c o", o=8),
                        in_=dram["gwT"][:, :].rearrange("(c p) o -> p c o", p=128)), DCE)
                    p.dma("gpsimd", lambda g: g.dma_start(
                        out=aTb[:].rearrange("p (c o) -> p c o", o=64),
                        in_=dram["aT"][:, :].rearrange("(c p) o -> p c o", p=128)), DCE)
                    p.dma("gpsimd", lambda g: g.dma_start(out=p8b[:], in_=dram["p8"][:, :]), DCE)
                    sems, _eln = _phase_ln(p, sbl, dram, "X1", "XN1", T, "E")
                    p.barrier(dma_sems=sems)
                    p.wait("tensor", DCE, DCE.cur)
                    sems = _phase_gate(p, sbl, psl, dram, T, gwTb, aTb, ident_b,
                                       p8b, hT)
                    p.barrier(dma_sems=sems + [DCE])

            if "F" in phases:
                with contextlib.ExitStack() as pctx:
                    sbl = lambda nm, shp, dt, **kw: pctx.enter_context(
                        nc.sbuf_tensor(nm, shp, dt, **kw))
                    psl = lambda nm, shp, dt=F32: pctx.enter_context(
                        nc.psum_tensor(nm, shp, dt))
                    sems = _phase_att1(p, sbl, psl, dram, T, c_colf, vb_b, maskT_b)
                    p.barrier(dma_sems=sems)

            if "G" in phases:
                with contextlib.ExitStack() as pctx:
                    sbl = lambda nm, shp, dt, **kw: pctx.enter_context(
                        nc.sbuf_tensor(nm, shp, dt, **kw))
                    psl = lambda nm, shp, dt=F32: pctx.enter_context(
                        nc.psum_tensor(nm, shp, dt))
                    epi, esems = _mk_epi_gate(p, sbl, dram, T, "G", "S1", "X1",
                                              "out", silu=True)
                    bFb = sbl("bFb", [64, D], BF16)
                    DCG = p.sem("DCG")
                    p.dma("gpsimd", lambda g: g.dma_start(out=bFb[:], in_=dram["bF"][:, :]), DCG)
                    p.wait("tensor", DCG, DCG.cur)
                    esems = esems + [DCG]
                    TH = T // 2

                    def lora_mm(t, th, os, tt, ps_ap):
                        return t.matmul(
                            ps_ap, hT[:, th * TH + tt * 128: th * TH + (tt + 1) * 128],
                            bFb[:, os * 512:(os + 1) * 512], start=False, stop=True)
                    sems = _phase_proj(p, sbl, psl, dram, T, "G", "XN1", "wbT",
                                       "out", epi, extra_mm=lora_mm)
                    p.barrier(dma_sems=sems + esems)

            if debug_outputs:
                DDB = p.sem("DDB")
                dbf = ctx.enter_context(nc.sbuf_tensor("dbgf", [128, max(T, D)], F32))
                dump = []
                if "A" in phases:
                    dump += [("dbg_kT", kT[:]), ("dbg_vtok", vtok[:])]
                if "PRO" in phases:
                    dump += [("dbg_ccol", c_col[:]), ("dbg_vbb", vb_b[:])]
                for nm, src_ap in dump:
                    n = src_ap.shape[1]
                    ms = p.step("vector", lambda v, src_ap=src_ap, n=n: v.tensor_copy(
                        dbf[:, 0:n], src_ap), wait=[(DDB, DDB.cur)])
                    p.dma("sync", lambda s, nm=nm, n=n: s.dma_start(
                        out=dram[nm][:, :], in_=dbf[:, 0:n]), DDB, wait=[(SV, ms)])
                if "E" in phases:
                    ms = p.step("vector", lambda v: v.tensor_copy(
                        dbf[0:64, 0:T], hT[:]), wait=[(DDB, DDB.cur)])
                    p.dma("sync", lambda s: s.dma_start(
                        out=dram["dbg_hT"][:, :], in_=dbf[0:64, 0:T]), DDB,
                        wait=[(SV, ms)])
                p.barrier(dma_sems=[DDB])

            for name in ENGINES:
                steps = p.q[name]
                if steps:
                    getattr(block, name)(
                        lambda engine, steps=steps: [f(engine) for f in steps])

    return nc, dram


def _phase_proj_atn(p, sb, psa, dram, T, tag, src, wname, epilogue):
    SP = p.esem["tensor"]
    TH = T // 2
    nTT = TH // 128
    SRC = dram[src]
    W = dram[wname]
    DXH = p.sem(f"{tag}DXH")
    DW = [p.sem(f"{tag}DW{i}") for i in range(2)]
    xnT = sb(f"{tag}xnT", [128, NCH * TH], BF16)
    wsl = [sb(f"{tag}w{i}", [128, NCH * 512], BF16) for i in range(2)]
    ps = [psa(f"{tag}ps{i}", [128, 512]) for i in range(4)]

    pe_th = 0
    pe_os = [0, 0]
    bank_free = [(None, 0)] * 4
    for th in range(2):
        d_xh = p.dma("gpsimd", lambda g, th=th: g.dma_start(
            out=xnT[:].rearrange("p (c t) -> p c t", t=TH),
            in_=SRC[:, th * TH:(th + 1) * TH].rearrange("(c p) t -> p c t", p=128)),
            DXH, wait=[(SP, pe_th)])
        for os in range(8):
            osg = th * 8 + os
            par = os % 2
            w = [(SP, pe_os[par])] if pe_os[par] else []
            d_w = p.dma("gpsimd", lambda g, os=os, par=par: g.dma_start(
                out=wsl[par][:].rearrange("p (k o) -> p k o", o=512),
                in_=W[:, os * 512:(os + 1) * 512].rearrange(
                    "(k p) o -> p k o", p=128)), DW[par], wait=w)
            for tt in range(nTT):
                b = (osg * nTT + tt) % 4
                fs, fv = bank_free[b]
                w0 = [(DW[par], d_w), (DXH, d_xh)]
                if fs is not None:
                    w0.append((fs, fv))
                for c in range(NCH):
                    p.step("tensor", lambda t, c=c, tt=tt, par=par, b=b: t.matmul(
                        ps[b][:], xnT[:, c * TH + tt * 128: c * TH + (tt + 1) * 128],
                        wsl[par][:, c * 512:(c + 1) * 512],
                        start=(c == 0), stop=(c == NCH - 1)),
                        wait=w0 if c == 0 else [])
                bank_free[b] = epilogue(p, th, os, tt, ps[b][:], SP.cur)
            pe_os[par] = SP.cur
        pe_th = SP.cur
    return [DXH, DW[0], DW[1]]



def host_constants():
    iota = (np.arange(16)[None, :] * 128 + np.arange(128)[:, None]).astype(np.float32)
    maskT = (np.arange(128)[:, None] <= np.arange(128)[None, :]).astype(np.float32)
    ident = np.eye(128, dtype=np.float32)
    p8 = np.kron(np.eye(8), np.ones((1, 8))).astype(np.float32)
    pp = np.arange(128)[:, None]
    jj = np.arange(2048)[None, :]
    rel = np.where(pp <= jj, (pp - jj).astype(np.float32), np.float32(-1e33))
    rel = rel.astype(np.float32)
    parr = np.arange(128)
    prow4 = np.stack([16.0 * (parr // 16), (parr % 16).astype(np.float32),
                      np.ones(128), np.ones(128)]).astype(np.float32)
    qarr = np.arange(2048)
    krow4 = np.stack([np.ones(2048), np.ones(2048),
                      -16.0 * (qarr // 16), -(qarr % 16).astype(np.float32)
                      ]).astype(np.float32)
    cmaskT = np.where(pp.T <= jj[:, :128].T, 0.0, -1e33).astype(np.float32).T
    cmaskT = np.where(np.arange(128)[:, None] >= np.arange(128)[None, :],
                      0.0, -1e33).astype(np.float32)
    cmaskT = np.where(np.arange(128)[None, :] <= np.arange(128)[:, None],
                      0.0, np.float32(-1e33)).astype(np.float32)
    return {"iota": iota, "maskT": maskT, "ident": ident, "p8": p8, "rel": rel,
            "prow4": prow4, "krow4": krow4, "cmaskT": cmaskT}


def prep_in_maps(inputs, T=2048, n_cores=N_CORES):
    x = np.ascontiguousarray(inputs["L0_attn_hidden_states"], dtype=np.float32)
    A = np.asarray(inputs["L1_ffn_A"])
    B = np.asarray(inputs["L1_ffn_B"])
    snz = np.asarray(inputs["L0_ffn_scales_and_zeros"])
    shared = {
        "wq": np.ascontiguousarray(inputs["L0_attn_Wq"], dtype=np.float32),
        "wkv": np.ascontiguousarray(inputs["L0_attn_Wkv"], dtype=np.float32),
        "wo": np.ascontiguousarray(inputs["L0_attn_Wo"], dtype=np.float32),
        "wqq": np.ascontiguousarray(inputs["L0_ffn_weight_q"], dtype=np.int32),
        "sT": np.ascontiguousarray(snz[:, :, 0].T, dtype=np.float32),
        "zT": np.ascontiguousarray(snz[:, :, 1].T, dtype=np.float32),
        "wbT": np.ascontiguousarray(np.asarray(inputs["L1_ffn_W_base"]).T,
                                    dtype=np.float32),
        "gwT": np.ascontiguousarray(np.asarray(inputs["L1_ffn_gate_W"]).T,
                                    dtype=np.float32),
        "aT": np.ascontiguousarray(A.reshape(64, D).T, dtype=np.float32),
        "bF": np.ascontiguousarray(np.transpose(B, (0, 2, 1)).reshape(64, D),
                                   dtype=np.float32),
        "bvq": np.ascontiguousarray(inputs["L1_attn_bv_q"], dtype=np.float32),
        "bvk": np.ascontiguousarray(inputs["L1_attn_bv_k"], dtype=np.float32),
        "bvv": np.ascontiguousarray(inputs["L1_attn_bv_v"], dtype=np.float32),
    }
    shared.update(host_constants())
    in_maps = []
    for b in range(n_cores):
        m = dict(shared)
        m["x0"] = np.ascontiguousarray(x[b, :T], dtype=np.float32)
        in_maps.append(m)
    return in_maps


def kernel(**inputs):
    T = 2048
    nc, _ = build_program(T=T)
    in_maps = prep_in_maps(inputs, T=T)
    res = run_bass_kernel_spmd(nc, in_maps, core_ids=list(range(N_CORES)))
    out = np.stack([np.asarray(r["out"], dtype=np.float32) for r in res.results],
                   axis=0)
    return out
